# version 24
# baseline (speedup 1.0000x reference)
import numpy as np
from contextlib import ExitStack
import concourse.bass as bass
import concourse.mybir as mybir
from concourse.bass_utils import run_bass_kernel_spmd

F32 = mybir.dt.float32
BF16 = mybir.dt.bfloat16
I32 = mybir.dt.int32
AF = mybir.ActivationFunctionType
ALU = mybir.AluOpType
AX = mybir.AxisListType

E = 1024
NCC = 8
H = 16
D = 64
CH = 128
RSQRT_C = float(0x5F3759DF)


class Build:
    def __init__(self, n_seq=4, T=2048):
        self.n_seq = n_seq
        self.T = T
        self.NC = T // CH
        nc = bass.Bass("TRN2", target_bir_lowering=False)
        self.nc = nc
        self.es = ExitStack()
        self.x_d = nc.dram_tensor("x", [n_seq, T, E], F32, kind="ExternalInput").ap()
        self.out_d = nc.dram_tensor("out", [n_seq, T, E], F32, kind="ExternalOutput").ap()
        self.wq_d = nc.dram_tensor("wqT", [E, 3 * E], F32, kind="ExternalInput").ap()
        self.wp_d = nc.dram_tensor("wpT", [E, E], F32, kind="ExternalInput").ap()
        self.wr_d = nc.dram_tensor("wrT", [E, E], F32, kind="ExternalInput").ap()
        self.wk_d = nc.dram_tensor("wkT", [E, E], F32, kind="ExternalInput").ap()
        self.wv_d = nc.dram_tensor("wvT", [E, E], F32, kind="ExternalInput").ap()
        self.mu_d = nc.dram_tensor("mu_in", [128, 16], F32, kind="ExternalInput").ap()
        self.id_d = nc.dram_tensor("ident_in", [128, 128], F32, kind="ExternalInput").ap()
        self.mask_d = nc.dram_tensor("mask4_in", [128, 512], F32, kind="ExternalInput").ap()

        sb = lambda name, shape, dt: self.es.enter_context(nc.sbuf_tensor(name, shape, dt))
        ps = lambda name, shape, dt: self.es.enter_context(nc.psum_tensor(name, shape, dt))
        self.wq = sb("wq", [128, NCC * 3 * E], BF16)
        self.wp = sb("wp", [128, NCC * E], BF16)
        self.wr = sb("wr", [128, NCC * E], BF16)
        self.wk = sb("wk", [128, NCC * E], BF16)
        self.wv = sb("wv", [128, NCC * E], BF16)
        self.mu = sb("mu", [128, 16], F32)
        self.ident = sb("ident", [128, 128], BF16)
        self.mask4 = sb("mask4", [128, 512], BF16)
        self.xin = [sb(f"xin{i}", [128, E], F32) for i in range(3)]
        self.obuf = sb("obuf", [128, E], F32)
        self.x1 = [sb(f"x1_{i}", [128, E], F32) for i in range(2)]
        self.stg = [sb(f"stg{i}", [128, E], BF16) for i in range(3)]
        self._stg_rr = 0
        self.xn1fm = sb("xn1fm", [128, NCC * 256], BF16)
        self.qfm = sb("qfm", [128, NCC * 256], BF16)
        self.kfm = sb("kfm", [128, NCC * 256], BF16)
        self.ktm = sb("ktm", [128, E], BF16)
        self.vcaug = sb("vcaug", [128, 1040], BF16)
        self.sct = sb("sct", [128, H * CH], BF16)
        self.ytm = sb("ytm", [128, E], BF16)
        self.yfm = sb("yfm", [128, E], BF16)
        self.S16s = [sb(f"S16_{i}", [128, 1040], BF16) for i in range(n_seq)]
        self.seqcol = sb("seqcol", [128, 8 * n_seq], BF16)
        self.xn3 = [sb(f"xn3_{i}", [128, NCC * 128], BF16) for i in range(2)]
        self.zc = sb("zc", [128, 1], BF16)
        self.dbuf = sb("dbuf", [128, E], BF16)
        self.xrfm = sb("xrfm", [128, E], BF16)
        self.xkfm = sb("xkfm", [128, NCC * 256], BF16)
        self.kact = sb("kact", [128, NCC * 256], BF16)
        self.scr = [sb(f"scr{i}", [128, 512], F32) for i in range(2)]
        self.mubc = sb("mubc", [128, 2 * E], BF16)
        self.ones128 = sb("ones128", [128, 128], BF16)
        self.consts = sb("consts", [128, 2], F32)
        self.rs = [sb(f"rs{i}", [128, E], BF16) for i in range(2)]
        self.stats = [sb(f"stats{i}", [128, 32], F32) for i in range(10)]
        self.pbt = [ps("pb0", [128, 512], F32), ps("pb1", [128, 512], F32),
                    ps("pw0", [128, 1024], F32), ps("pw1", [128, 1024], F32),
                    ps("pb2", [128, 512], F32)]
        self.slots = [(0, 0), (1, 0), (2, 0), (2, 512), (3, 0), (3, 512), (4, 0)]
        self.tb = [ps(f"tb{i}", [128, 1024], BF16) for i in range(1)]
        keys = (["t", "v", "s"] + [f"lq{i}" for i in range(4)]
                + [f"sg{i}" for i in range(4)] + [f"w{i}" for i in range(8)]
                + [f"ts{i}" for i in range(4)])
        self.sems = {k: self.es.enter_context(nc.semaphore(f"sem_{k}")) for k in keys}
        self.cnt = {k: 0 for k in self.sems}
        self.mult = {k: (1 if k in ("t", "v", "s") else 16) for k in keys}
        self.dma_engine = {k: ("q" if k.startswith("lq") else "g")
                           for k in keys if k not in ("t", "v", "s")}
        self.regions = {}
        self.waited = {a: {b: 0 for b in self.sems} for a in self.sems}
        self.pend = {k: False for k in self.sems}
        self._bank_rr = 0
        self._tmp_rr = 0
        self._scr_rr = 0
        self.ops = []
        self.final_waits = []

    def _reg(self, name):
        r = self.regions.get(name)
        if r is None:
            r = {"w": None, "r": set()}
            self.regions[name] = r
        return r

    def _need_wait(self, ek, pe, kind):
        if pe == ek and ek == "t":
            return False
        return True

    def emit(self, ek, fn, reads=(), writes=(), inc=True, unit=None, cost=None):
        oid = len(self.ops)
        if ek not in ("t", "v", "s"):
            writes = list(writes) + [f"_q_{ek}"]
        deps = {}
        def add(pid, kind):
            if pid is None or pid == oid:
                return
            if kind == "raw" or pid not in deps:
                deps[pid] = kind
        for nm in reads:
            r = self._reg(nm)
            add(r["w"], "raw")
        for nm in writes:
            r = self._reg(nm)
            add(r["w"], "waw")
            for pid in r["r"]:
                add(pid, "war")
        if cost is None:
            cost = {"t": 150, "v": 330, "s": 650}.get(ek, 1400)
        self.ops.append(dict(id=oid, ek=ek, fn=fn, inc=inc, deps=deps,
                             unit=unit, cost=cost))
        for nm in writes:
            self.regions[nm] = {"w": oid, "r": set()}
        for nm in reads:
            self._reg(nm)["r"].add(oid)

    def take_pb(self):
        ti, off = self.slots[self._bank_rr % 7]
        self._bank_rr += 1
        return self.pbt[ti], off, f"pb{ti}_{off // 512}"

    def utok(self):
        self._utok = getattr(self, "_utok", 0) + 1
        return f"u{self._utok}"

    def take_scr(self):
        t = self._scr_rr % len(self.scr)
        self._scr_rr += 1
        return t

    def take_stg(self):
        t = self._stg_rr % len(self.stg)
        self._stg_rr += 1
        return t


    @staticmethod
    def _phys(ek):
        if ek in ("t", "v", "s"):
            return ek
        return "q" if (ek.startswith("lq") or ek.startswith("ts")) else "g"

    def schedule(self):
        import heapq
        ops = self.ops
        units = []
        op_unit = [0] * len(ops)
        for op in ops:
            tok = op["unit"]
            if (tok is not None and units and units[-1]["tok"] == tok
                    and units[-1]["ek"] == op["ek"]):
                u = units[-1]
            else:
                u = dict(idx=len(units), tok=tok, ek=op["ek"], ops=[], cost=0.0)
                units.append(u)
            u["ops"].append(op)
            u["cost"] += op["cost"]
            op_unit[op["id"]] = u["idx"]
        n = len(units)
        succs = [[] for _ in range(n)]
        for u in units:
            d = {}
            for op in u["ops"]:
                for pid, kind in op["deps"].items():
                    pu = op_unit[pid]
                    if pu != u["idx"]:
                        if kind == "raw" or pu not in d:
                            d[pu] = kind
            u["deps"] = d
            for pu in d:
                succs[pu].append(u["idx"])
        LAT = 250.0
        prio = [0.0] * n
        for i in range(n - 1, -1, -1):
            m = 0.0
            for sx in succs[i]:
                m = max(m, prio[sx])
            prio[i] = units[i]["cost"] + m + (LAT if succs[i] else 0.0)
        ndep = [len(u["deps"]) for u in units]
        dep_fin = [0.0] * n
        finish = [0.0] * n
        free = {"t": 0.0, "v": 0.0, "s": 0.0, "q": 0.0, "g": 0.0}
        ready = [(-prio[i], i) for i in range(n) if ndep[i] == 0]
        heapq.heapify(ready)
        order = []
        K = 40
        while ready:
            cand = []
            best = None
            best_key = None
            for _ in range(min(K, len(ready))):
                item = heapq.heappop(ready)
                i = item[1]
                pe = self._phys(units[i]["ek"])
                est = max(free[pe], dep_fin[i])
                key = (est, item[0])
                if best is None or key < best_key:
                    if best is not None:
                        cand.append(best)
                    best, best_key = item, key
                else:
                    cand.append(item)
            for it in cand:
                heapq.heappush(ready, it)
            i = best[1]
            u = units[i]
            pe = self._phys(u["ek"])
            start = max(free[pe], dep_fin[i])
            f = start + u["cost"]
            finish[i] = f
            free[pe] = f
            order.append(i)
            for sx in succs[i]:
                ndep[sx] -= 1
                dep_fin[sx] = max(dep_fin[sx], f + LAT)
                if ndep[sx] == 0:
                    heapq.heappush(ready, (-prio[sx], sx))
        assert len(order) == n, f"scheduled {len(order)} of {n} units (cycle?)"
        self.sched_makespan_us = max(finish) / 1000 if n else 0
        key_cnt = {k: 0 for k in self.sems}
        mark = [None] * n
        pw = {e: {k: 0 for k in self.sems} for e in ("t", "v", "s", "q", "g")}
        sched_ops = []
        for i in order:
            u = units[i]
            pe = self._phys(u["ek"])
            req = {}
            for pu, kind in u["deps"].items():
                mk = mark[pu]
                assert mk is not None
                sk, cn = mk
                if not self._need_wait(u["ek"], sk, kind):
                    continue
                if cn > req.get(sk, 0):
                    req[sk] = cn
            waits = [(sk, cn) for sk, cn in req.items() if cn > pw[pe][sk]]
            for sk, cn in waits:
                pw[pe][sk] = cn
            first = True
            for op in u["ops"]:
                sched_ops.append((op["ek"], op["fn"], waits if first else [], op["inc"]))
                first = False
                if op["inc"]:
                    key_cnt[op["ek"]] += 1
            assert u["ops"][-1]["inc"], "unit must end with an inc'd op"
            mark[i] = (u["ek"], key_cnt[u["ek"]])
        self.sched_ops = sched_ops
        self.final_waits = [("gpsimd", f"sg{i}", key_cnt[f"sg{i}"]) for i in range(4)]

    def startup(self):
        em = self.emit
        def wload(dst, src, cols):
            src_r = src.rearrange("(cc p) d -> p cc d", p=128)
            dst_r = dst.ap().rearrange("p (cc d) -> p cc d", d=cols)
            return src_r, dst_r
        for i, (dst, src, cols, nm) in enumerate(((self.wq, self.wq_d, 3 * E, "wq"),
                                                   (self.wp, self.wp_d, E, "wp"),
                                                   (self.wr, self.wr_d, E, "wr"),
                                                   (self.wk, self.wk_d, E, "wk"),
                                                   (self.wv, self.wv_d, E, "wv"))):
            s_r, d_r = wload(dst, src, cols)
            em(f"w{i}", lambda e, d_r=d_r, s_r=s_r: e.dma_start(out=d_r, in_=s_r),
               writes=[nm])
        em("w5", lambda e: e.dma_start(out=self.mu[:], in_=self.mu_d), writes=["mu"])
        em("w6", lambda e: e.dma_start(out=self.ident[:], in_=self.id_d), writes=["ident"])
        em("w7", lambda e: e.dma_start(out=self.mask4[:], in_=self.mask_d), writes=["mask4"])
        ones_ap = self.vcaug.ap().rearrange("p (h x) -> p h x", x=65)[:, :, 64:65]
        em("v", lambda e: e.memset(ones_ap, 1.0), writes=["vcaug_ones"])
        em("v", lambda e: e.memset(self.ones128[:], 1.0), writes=["ones128"])
        em("v", lambda e: e.memset(self.consts[:, 0:1], 1.5), writes=["consts"])
        em("v", lambda e: e.memset(self.zc[:], 0.0), writes=["zc"])
        for m in range(NCC):
            em("v", lambda e, m=m: e.tensor_scalar(
                self.mubc[:, 128 * m:128 * m + 128], self.ones128[:],
                self.mu[:, m:m + 1], None, ALU.mult),
               reads=["ones128", "mu"], writes=["mubc"])
            em("v", lambda e, m=m: e.tensor_scalar(
                self.mubc[:, E + 128 * m:E + 128 * m + 128], self.ones128[:],
                self.mu[:, 8 + m:8 + m + 1], None, ALU.mult),
               reads=["ones128", "mu"], writes=["mubc"])
        em("s", lambda e: e.preload_act_table(AF.Exp), inc=True)

    def ln(self, src_ap_halves, src_regions, st_i, extra_reads=()):
        em = self.emit
        st = self.stats[st_i]
        nm = f"st{st_i}"
        reads = list(src_regions) + list(extra_reads)
        n = len(src_ap_halves)
        for i, h in enumerate(src_ap_halves):
            em("v", lambda e, i=i, h=h: e.bn_stats(st[:, 6 * i:6 * i + 6], h),
               reads=reads, writes=[nm] if i == 0 else [nm], inc=True)
        em("v", lambda e: e.bn_aggr(st[:, 24:26], st[:, 0:6 * n]), reads=[nm], writes=[nm])
        z, y, t, u = st[:, 26:27], st[:, 27:28], st[:, 28:29], st[:, 29:30]
        em("v", lambda e: e.tensor_scalar(z, st[:, 25:26], 1e-5, None, ALU.add),
           reads=[nm], writes=[nm])
        em("v", lambda e: e.tensor_scalar(y.bitcast(I32), z.bitcast(I32), -0.5,
                                          RSQRT_C, ALU.mult, ALU.add),
           reads=[nm], writes=[nm])
        for _ in range(2):
            em("v", lambda e: e.tensor_scalar(t, y, y, z, ALU.mult, ALU.mult),
               reads=[nm], writes=[nm], cost=300)
            em("v", lambda e: e.tensor_scalar(u, t, -0.5, 1.5, ALU.mult, ALU.add),
               reads=[nm], writes=[nm], cost=300)
            em("v", lambda e: e.tensor_scalar(y, u, y, None, ALU.mult),
               reads=[nm], writes=[nm], cost=300)
        em("v", lambda e: e.scalar_tensor_tensor(st[:, 30:31], y, -1.0, st[:, 24:25],
                                                 ALU.mult, ALU.mult),
           reads=[nm], writes=[nm])
        return y, st[:, 30:31], nm


    def ln_pair(self, halves0, regs0, halves1, regs1, st_i):
        em = self.emit
        st = self.stats[st_i]
        nm = f"st{st_i}"
        for base, halves, regs in ((0, halves0, regs0), (12, halves1, regs1)):
            for i, h in enumerate(halves):
                em("v", lambda e, i=i, h=h, base=base: e.bn_stats(
                    st[:, base + 6 * i:base + 6 * i + 6], h),
                   reads=list(regs), writes=[nm], cost=700)
            em("v", lambda e, base=base: e.bn_aggr(
                st[:, 24 + base // 6:26 + base // 6], st[:, base:base + 12]),
               reads=[nm], writes=[nm], cost=300)
        mean2 = st[:, 24:28].rearrange("p (a b) -> p a b", b=2)[:, :, 0:1]
        var2 = st[:, 24:28].rearrange("p (a b) -> p a b", b=2)[:, :, 1:2]
        z, y, t, u = st[:, 28:30], st[:, 30:32], st[:, 20:22], st[:, 22:24]
        em("v", lambda e, var2=var2: e.tensor_scalar(z, var2, 1e-5, None, ALU.add),
           reads=[nm], writes=[nm], cost=300)
        em("v", lambda e: e.tensor_scalar(y.bitcast(I32), z.bitcast(I32), -0.5,
                                          RSQRT_C, ALU.mult, ALU.add),
           reads=[nm], writes=[nm], cost=300)
        for _ in range(2):
            em("v", lambda e: e.tensor_mul(t, y, y), reads=[nm], writes=[nm], cost=300)
            em("v", lambda e: e.tensor_mul(t, t, z), reads=[nm], writes=[nm], cost=300)
            em("v", lambda e: e.tensor_scalar(u, t, -0.5, 1.5, ALU.mult, ALU.add),
               reads=[nm], writes=[nm], cost=300)
            em("v", lambda e: e.tensor_mul(y, y, u), reads=[nm], writes=[nm], cost=300)
        em("v", lambda e, mean2=mean2: e.scalar_tensor_tensor(
            st[:, 18:20], y, -1.0, mean2, ALU.mult, ALU.mult),
           reads=[nm], writes=[nm], cost=300)
        return ((y[:, 0:1], st[:, 18:19]), (y[:, 1:2], st[:, 19:20]), nm)

    def load_chunk(self, s, c, k):
        xb = self.xin[k % 3]
        src = self.x_d[s, c * CH:(c + 1) * CH, :]
        self.emit(f"lq{k % 3}", lambda e: e.dma_start(out=xb[:], in_=src),
                  writes=[f"xin{k % 3}"], cost=1700)

    def ln1_and_transpose(self, s, g, kb):
        em = self.emit
        xbs = [self.xin[(kb + i) % 3] for i in range(2)]
        xrs = [f"xin{(kb + i) % 3}" for i in range(2)]
        pr0, pr1, stnm = self.ln_pair(
            [xbs[0][:, 0:512], xbs[0][:, 512:1024]], [xrs[0]],
            [xbs[1][:, 0:512], xbs[1][:, 512:1024]], [xrs[1]], 0 + kb % 2)
        for i, (rstd, bias) in enumerate((pr0, pr1)):
            xb, xr = xbs[i], xrs[i]
            si = self.take_stg()
            xt = self.stg[si]
            xtnm = f"stg{si}"
            em("s", lambda e, xt=xt, xb=xb, rstd=rstd, bias=bias: e.activation(
                xt[:], xb[:], AF.Identity, bias=bias, scale=rstd),
               reads=[xr, stnm], writes=[xtnm], cost=1100)
            off = 128 * i
            out_ap = self.xn1fm.ap().rearrange("p (cc t) -> p cc t", t=256)[:, :, off:off + 128]
            in_ap = xt.ap().rearrange("p (cc t) -> p cc t", t=128)
            em(f"ts{(kb + i) % 2}", lambda e, out_ap=out_ap, in_ap=in_ap:
               e.dma_start_transpose(out_ap, in_ap),
               reads=[xtnm], writes=[f"xn1fm_{i}"], cost=1900)


    def qk_group(self, s, g):
        em = self.emit
        for tensor_i, (dst, dstnm) in enumerate(((self.qfm, "qfm"), (self.kfm, "kfm"))):
            for pair in range(NCC // 2):
                bank, boff, bnm = self.take_pb()
                tok = self.utok()
                for half in range(2):
                    m = 2 * pair + half
                    for cc in range(NCC):
                        lhsT = self.wq[:, cc * 3072 + tensor_i * E + 128 * m:
                                       cc * 3072 + tensor_i * E + 128 * m + 128]
                        rhs = self.xn1fm[:, cc * 256:cc * 256 + 256]
                        em("t", lambda e, o=bank[:, boff + 256 * half:boff + 256 * half + 256],
                           l=lhsT, r=rhs, cc=cc:
                           e.matmul(o, l, r, start=(cc == 0), stop=(cc == NCC - 1)),
                           reads=["wq", "xn1fm_0", "xn1fm_1"], writes=[bnm],
                           inc=(cc == NCC - 1), unit=tok, cost=145)
                rg = [f"{dstnm}_{2 * pair}", f"{dstnm}_{2 * pair + 1}"]
                s0 = self.take_scr()
                r0 = self.scr[s0]
                em("s", lambda e, dst=dst, pair=pair, bank=bank, boff=boff: e.activation(
                    dst[:, 512 * pair:512 * pair + 512], bank[:, boff:boff + 512], AF.Relu),
                   reads=[bnm], writes=rg, cost=600)
                em("v", lambda e, r0=r0, dst=dst, pair=pair, bank=bank, boff=boff:
                   e.tensor_tensor(r0[:], bank[:, boff:boff + 512],
                                   dst[:, 512 * pair:512 * pair + 512], ALU.subtract),
                   reads=[bnm] + rg, writes=[f"scr{s0}"], cost=560)
                em("s", lambda e, r0=r0: e.activation(r0[:], r0[:], AF.Exp),
                   reads=[f"scr{s0}"], writes=[f"scr{s0}"], cost=600)
                em("v", lambda e, r0=r0, dst=dst, pair=pair: e.tensor_add(
                    dst[:, 512 * pair:512 * pair + 512],
                    dst[:, 512 * pair:512 * pair + 512], r0[:]),
                   reads=[f"scr{s0}"] + rg, writes=rg, cost=430)

    def v_chunk(self, s, c):
        em = self.emit
        off = 128 * (c % 2)
        for half in range(2):
            bank, boff, bnm = self.take_pb()
            tok = self.utok()
            for cc in range(NCC):
                lhsT = self.xn1fm[:, cc * 256 + off:cc * 256 + off + 128]
                rhs = self.wq[:, cc * 3072 + 2 * E + half * 512:
                              cc * 3072 + 2 * E + half * 512 + 512]
                em("t", lambda e, o=bank[:, boff:boff + 512], l=lhsT, r=rhs, cc=cc:
                   e.matmul(o, l, r, start=(cc == 0), stop=(cc == NCC - 1)),
                   reads=["wq", f"xn1fm_{c % 2}"], writes=[bnm], inc=(cc == NCC - 1),
                   unit=tok, cost=245)
            in_ap = bank[:, boff:boff + 512].rearrange("p (h x) -> p h x", x=64)
            out_ap = self.vcaug.ap().rearrange("p (h x) -> p h x", x=65)[
                :, 8 * half:8 * half + 8, 0:64]
            em("s", lambda e, out_ap=out_ap, in_ap=in_ap: e.copy(out_ap, in_ap),
               reads=[bnm], writes=[f"vcaug_{half}"])

    def ktm_chunk(self, s, c):
        em = self.emit
        off = 128 * (c % 2)
        tbank = self.tb[0]
        tok = self.utok()
        for m in range(NCC):
            em("t", lambda e, m=m: e.transpose(tbank[:, 128 * m:128 * m + 128],
                                               self.kfm[:, 256 * m + off:256 * m + off + 128],
                                               self.ident[:]),
               reads=[f"kfm_{m}", "ident"], writes=["tb0"], inc=(m == NCC - 1),
               unit=tok, cost=130)
        em("v", lambda e, tbank=tbank: e.tensor_copy(self.ktm[:], tbank[:, 0:1024]),
           reads=["tb0"], writes=["ktm"], cost=700)

    def attn_chunk(self, s, c, k):
        em = self.emit
        off = 128 * (c % 2)
        S16 = self.S16s[s]
        snm = f"S16_{s}"

        def qk_slice(t, h):
            base = 64 * (h % 2)
            col = 256 * (h // 2) + off
            return t[base:base + 64, col:col + 128]

        for heads in ((0, 2, 4, 6), (1, 3, 5, 7), (8, 10, 12, 14), (9, 11, 13, 15)):
            bank, boff, bnm = self.take_pb()
            tok = self.utok()
            for j, h in enumerate(heads):
                em("t", lambda e, h=h, j=j, bank=bank, boff=boff: e.matmul(
                    bank[:, boff + 128 * j:boff + 128 * j + 128],
                    qk_slice(self.kfm, h), qk_slice(self.qfm, h),
                    start=True, stop=True),
                   reads=[f"kfm_{h // 2}", f"qfm_{h // 2}"], writes=[bnm], inc=(j == 3),
                   unit=tok, cost=100)
            h0 = heads[0]
            out_ap = self.sct.ap().rearrange("p (a b) -> p a b", b=128)[:, h0:h0 + 7:2, :]
            em("v", lambda e, out_ap=out_ap, bank=bank, boff=boff: e.tensor_mul(
                out_ap, bank[:, boff:boff + 512], self.mask4[:]),
               reads=[bnm, "mask4"], writes=[f"sct{h0}"])
        first = (c == 0)
        for half in range(2):
            bank = self.pbt[2 + half]
            rg0, rg1 = f"pb{2 + half}_0", f"pb{2 + half}_1"
            tok = self.utok()
            for j in range(8):
                h = 8 * half + j
                ocol = 512 * (j // 4) + 66 * (j % 4)
                wnm = rg0 if j < 4 else rg1
                sctnm = f"sct{(h % 2) if h < 8 else 8 + (h % 2)}"
                vcnm = f"vcaug_{h // 8}"
                em("t", lambda e, h=h, ocol=ocol, bank=bank: e.matmul(
                    bank[:, ocol:ocol + 65],
                    self.sct[:, 128 * h:128 * h + 128],
                    self.vcaug[:, 65 * h:65 * h + 65],
                    start=True, stop=first),
                   reads=[sctnm, vcnm, "vcaug_ones"], writes=[wnm],
                   inc=(first and j == 7), unit=tok, cost=100)
                if not first:
                    base = 64 * (h % 2)
                    scol = 130 * (h // 2) + 65 * (h % 2)
                    em("t", lambda e, h=h, ocol=ocol, base=base, scol=scol, bank=bank: e.matmul(
                        bank[:, ocol:ocol + 65],
                        qk_slice(self.qfm, h),
                        S16[base:base + 64, scol:scol + 65],
                        start=False, stop=True),
                       reads=[f"qfm_{h // 2}", snm], writes=[wnm], inc=(j == 7),
                       unit=tok, cost=100)
            st = self.stats[8 + c % 2]
            stnm4 = f"st{8 + c % 2}"
            den_ap = bank.ap().rearrange("p (a x) -> p a x", x=512)[:, :, 0:264].rearrange(
                "p a (j x) -> p a j x", x=66)[:, :, :, 64:65]
            em("v", lambda e, den_ap=den_ap, st=st, half=half: e.tensor_scalar(
                st[:, 8 * half:8 * half + 8], den_ap, 1e-6, None, ALU.add),
               reads=[rg0, rg1], writes=[stnm4], cost=310)
            em("v", lambda e, st=st, half=half: e.reciprocal(
                st[:, 16 + 8 * half:24 + 8 * half], st[:, 8 * half:8 * half + 8]),
               reads=[stnm4], writes=[stnm4], cost=310)
            num_ap = bank.ap().rearrange("p (a x) -> p a x", x=512)[:, :, 0:264].rearrange(
                "p a (j x) -> p a j x", x=66)[:, :, :, 0:64]
            rec_ap = st[:, 16 + 8 * half:24 + 8 * half].unsqueeze(2).broadcast_to([128, 8, 64])
            em("v", lambda e, half=half, num_ap=num_ap, rec_ap=rec_ap: e.tensor_tensor(
                self.ytm.ap().rearrange("p (j x) -> p j x", x=64)[
                    :, 8 * half:8 * half + 8, :],
                num_ap, rec_ap, ALU.mult),
               reads=[rg0, rg1, stnm4], writes=["ytm"], cost=700)
        for pg, pairs in enumerate(((0, 1, 2), (3, 4, 5), (6, 7))):
            bank, boff, bnm = self.take_pb()
            tok = self.utok()
            for i, p in enumerate(pairs):
                em("t", lambda e, p=p, i=i, bank=bank, boff=boff: e.matmul(
                    bank[:, boff + 130 * i:boff + 130 * i + 130],
                    self.ktm[:, 128 * p:128 * p + 128],
                    self.vcaug[:, 130 * p:130 * p + 130],
                    start=True, stop=True),
                   reads=["ktm", f"vcaug_{p // 4}", "vcaug_ones"], writes=[bnm],
                   inc=(i == len(pairs) - 1), unit=tok, cost=120)
            w = 130 * len(pairs)
            colS = 130 * pairs[0]
            if first:
                em("v", lambda e, w=w, colS=colS, bank=bank, boff=boff, S16=S16: e.tensor_copy(
                    S16[:, colS:colS + w], bank[:, boff:boff + w]),
                   reads=[bnm], writes=[snm], cost=500)
            else:
                em("v", lambda e, w=w, colS=colS, bank=bank, boff=boff, S16=S16: e.tensor_add(
                    S16[:, colS:colS + w], S16[:, colS:colS + w],
                    bank[:, boff:boff + w]),
                   reads=[bnm, snm], writes=[snm], cost=560)
        em("ts3", lambda e: e.dma_start_transpose(
            self.yfm.ap().rearrange("p (m f) -> p m f", f=128),
            self.ytm.ap().rearrange("p (m f) -> p m f", f=128)),
           reads=["ytm"], writes=["yfm"], cost=1900)

    def proj_ln2(self, s, c, k):
        em = self.emit
        sl0, sl1 = self.take_pb(), self.take_pb()
        for half, sl in ((0, sl0), (1, sl1)):
            bank, boff, bnm = sl
            tok = self.utok()
            for m in range(NCC):
                em("t", lambda e, m=m, half=half, o=bank[:, boff:boff + 512]: e.matmul(
                    o, self.yfm[:, 128 * m:128 * m + 128],
                    self.wp[:, m * E + half * 512:m * E + half * 512 + 512],
                    start=(m == 0), stop=(m == NCC - 1)),
                   reads=["yfm", "wp"], writes=[bnm], inc=(m == NCC - 1),
                   unit=tok, cost=245)
        rstd, bias, stnm = self.ln([sl0[0][:, sl0[1]:sl0[1] + 512],
                                    sl1[0][:, sl1[1]:sl1[1] + 512]],
                                   [sl0[2], sl1[2]], 2 + k % 2)
        x1b = self.x1[k % 2]
        em("s", lambda e: e.activation(x1b[:, 0:512], sl0[0][:, sl0[1]:sl0[1] + 512],
                                       AF.Identity, bias=bias, scale=rstd),
           reads=[sl0[2], stnm], writes=[f"x1_{k % 2}"])
        em("s", lambda e: e.activation(x1b[:, 512:1024], sl1[0][:, sl1[1]:sl1[1] + 512],
                                       AF.Identity, bias=bias, scale=rstd),
           reads=[sl1[2], stnm], writes=[f"x1_{k % 2}"])
        em("v", lambda e: e.tensor_add(x1b[:], x1b[:], self.xin[k % 3][:]),
           reads=[f"x1_{k % 2}", f"xin{k % 3}"], writes=[f"x1_{k % 2}"])

    def ln3_pair(self, s, kb):
        b0, b1 = self.x1[kb % 2], self.x1[(kb + 1) % 2]
        r0, r1 = f"x1_{kb % 2}", f"x1_{(kb + 1) % 2}"
        pr0, pr1, stnm = self.ln_pair(
            [b0[:, 0:512], b0[:, 512:1024]], [r0],
            [b1[:, 0:512], b1[:, 512:1024]], [r1], 4 + kb % 2)
        self._ln3 = (pr0, pr1, stnm)

    def ln3_mix(self, s, c, k):
        em = self.emit
        x1b = self.x1[k % 2]
        x1r = f"x1_{k % 2}"
        (rstd, bias) = self._ln3[c % 2]
        stnm = self._ln3[2]
        si = self.take_stg()
        xt = self.stg[si]
        xtnm = f"stg{si}"
        em("s", lambda e: e.activation(xt[:], x1b[:], AF.Identity, bias=bias, scale=rstd),
           reads=[x1r, stnm], writes=[xtnm], cost=1100)
        xb = self.xn3[k % 2]
        xbnm = f"xn3_{k % 2}"
        em("ts2", lambda e, xt=xt, xb=xb: e.dma_start_transpose(
            xb.ap().rearrange("p (m f) -> p m f", f=128),
            xt.ap().rearrange("p (m f) -> p m f", f=128)),
           reads=[xtnm], writes=[xbnm], cost=1900)
        cur = xb.ap().rearrange("p (m f) -> p m f", f=128)
        dr = self.dbuf.ap().rearrange("p (m f) -> p m f", f=128)
        if c == 0:
            pcol = self.zc.ap().unsqueeze(1).broadcast_to([128, NCC, 1])
            prd = ["zc"]
        else:
            pcol = self.seqcol[:, 8 * s:8 * s + 8].unsqueeze(2)
            prd = [f"seqcol{s}"]
        em("v", lambda e: e.tensor_tensor(dr[:, :, 1:128], cur[:, :, 1:128],
                                          cur[:, :, 0:127], ALU.subtract),
           reads=[xbnm], writes=["dbuf"], cost=700)
        em("v", lambda e, pcol=pcol: e.tensor_tensor(
            dr[:, :, 0:1], cur[:, :, 0:1], pcol, ALU.subtract),
           reads=[xbnm] + prd, writes=["dbuf"], cost=300)
        off = 128 * (c % 2)
        xk_ap = self.xkfm.ap().rearrange("p (cc t) -> p cc t", t=256)[:, :, off:off + 128]
        xr_r = self.xrfm.ap().rearrange("p (m f) -> p m f", f=128)
        em("v", lambda e: e.tensor_mul(self.xrfm[:], self.dbuf[:], self.mubc[:, 0:E]),
           reads=["dbuf", "mubc"], writes=["xrfm"], cost=700)
        em("v", lambda e, xr_r=xr_r, cur=cur: e.tensor_tensor(
            xr_r[:, :, 1:128], xr_r[:, :, 1:128], cur[:, :, 0:127], ALU.add),
           reads=["xrfm", xbnm], writes=["xrfm"], cost=700)
        em("v", lambda e, xr_r=xr_r, pcol=pcol: e.tensor_tensor(
            xr_r[:, :, 0:1], xr_r[:, :, 0:1], pcol, ALU.add),
           reads=["xrfm"] + prd, writes=["xrfm"], cost=300)
        em("v", lambda e, xk_ap=xk_ap: e.tensor_tensor(
            xk_ap, self.dbuf[:], self.mubc[:, E:2 * E], ALU.mult),
           reads=["dbuf", "mubc"], writes=["xkfm"], cost=700)
        em("v", lambda e, xk_ap=xk_ap, cur=cur: e.tensor_tensor(
            xk_ap[:, :, 1:128], xk_ap[:, :, 1:128], cur[:, :, 0:127], ALU.add),
           reads=["xkfm", xbnm], writes=["xkfm"], cost=700)
        em("v", lambda e, xk_ap=xk_ap, pcol=pcol: e.tensor_tensor(
            xk_ap[:, :, 0:1], xk_ap[:, :, 0:1], pcol, ALU.add),
           reads=["xkfm"] + prd, writes=["xkfm"], cost=300)
        em("v", lambda e, s=s, cur=cur: e.tensor_copy(
            self.seqcol[:, 8 * s:8 * s + 8].unsqueeze(2), cur[:, :, 127:128]),
           reads=[xbnm], writes=[f"seqcol{s}"], cost=300)

    def r_chunk(self, s, c, k):
        em = self.emit
        sl0, sl1 = self.take_pb(), self.take_pb()
        for half, sl in ((0, sl0), (1, sl1)):
            bank, boff, bnm = sl
            tok = self.utok()
            for m in range(NCC):
                em("t", lambda e, m=m, half=half, o=bank[:, boff:boff + 512]: e.matmul(
                    o, self.xrfm[:, 128 * m:128 * m + 128],
                    self.wr[:, m * E + half * 512:m * E + half * 512 + 512],
                    start=(m == 0), stop=(m == NCC - 1)),
                   reads=["xrfm", "wr"], writes=[bnm], inc=(m == NCC - 1),
                   unit=tok, cost=245)
        rb = self.rs[k % 2]
        for half, sl in ((0, sl0), (1, sl1)):
            em("s", lambda e, half=half, sl=sl, rb=rb: e.activation(
                rb[:, 512 * half:512 * half + 512], sl[0][:, sl[1]:sl[1] + 512],
                AF.Tanh, scale=0.5),
               reads=[sl[2]], writes=[f"rs{k % 2}"])
        em("v", lambda e, rb=rb: e.tensor_scalar(rb[:], rb[:], 0.5, 0.5, ALU.mult, ALU.add),
           reads=[f"rs{k % 2}"], writes=[f"rs{k % 2}"])

    def kff_group(self, s, g):
        em = self.emit
        for pair in range(NCC // 2):
            bank, boff, bnm = self.take_pb()
            tok = self.utok()
            for half in range(2):
                m = 2 * pair + half
                for cc in range(NCC):
                    em("t", lambda e, m=m, cc=cc,
                       o=bank[:, boff + 256 * half:boff + 256 * half + 256]: e.matmul(
                        o, self.wk[:, cc * E + 128 * m:cc * E + 128 * m + 128],
                        self.xkfm[:, cc * 256:cc * 256 + 256],
                        start=(cc == 0), stop=(cc == NCC - 1)),
                       reads=["wk", "xkfm"], writes=[bnm], inc=(cc == NCC - 1),
                       unit=tok, cost=145)
            s0 = self.take_scr()
            r0 = self.scr[s0]
            em("s", lambda e, r0=r0, bank=bank, boff=boff: e.activation(
                r0[:], bank[:, boff:boff + 512], AF.Relu),
               reads=[bnm], writes=[f"scr{s0}"])
            em("v", lambda e, r0=r0, pair=pair: e.tensor_mul(
                self.kact[:, 512 * pair:512 * pair + 512], r0[:], r0[:]),
               reads=[f"scr{s0}"], writes=["kact"])

    def vff_ln4(self, s, c, k):
        em = self.emit
        off = 128 * (c % 2)
        sl0, sl1 = self.take_pb(), self.take_pb()
        for half, sl in ((0, sl0), (1, sl1)):
            bank, boff, bnm = sl
            tok = self.utok()
            for m in range(NCC):
                em("t", lambda e, m=m, half=half, o=bank[:, boff:boff + 512]: e.matmul(
                    o, self.kact[:, 256 * m + off:256 * m + off + 128],
                    self.wv[:, m * E + half * 512:m * E + half * 512 + 512],
                    start=(m == 0), stop=(m == NCC - 1)),
                   reads=["kact", "wv"], writes=[bnm], inc=(m == NCC - 1),
                   unit=tok, cost=245)
        rb = self.rs[k % 2]
        ob = self.obuf
        for half, sl in ((0, sl0), (1, sl1)):
            em("v", lambda e, half=half, sl=sl, rb=rb: e.tensor_mul(
                ob[:, 512 * half:512 * half + 512], rb[:, 512 * half:512 * half + 512],
                sl[0][:, sl[1]:sl[1] + 512]),
               reads=[f"rs{k % 2}", sl[2]], writes=["obuf"], cost=1100)
        rstd, bias, stnm = self.ln([ob[:, 0:512], ob[:, 512:1024]], ["obuf"], 6 + k % 2)
        em("s", lambda e: e.activation(ob[:], ob[:], AF.Identity,
                                       bias=bias, scale=rstd),
           reads=["obuf", stnm], writes=["obuf"])
        em("v", lambda e: e.tensor_add(ob[:], ob[:], self.x1[k % 2][:]),
           reads=["obuf", f"x1_{k % 2}"], writes=["obuf"])
        dst = self.out_d[s, c * CH:(c + 1) * CH, :]
        self.emit(f"sg{k % 2}", lambda e: e.dma_start(out=dst, in_=ob[:]),
                  reads=["obuf"], cost=1700)

    def stage_a(self, s, g):
        kb = 2 * (g * self.n_seq + s)
        c0, c1 = 2 * g, 2 * g + 1
        self.load_chunk(s, c0, kb)
        self.load_chunk(s, c1, kb + 1)
        self.ln1_and_transpose(s, g, kb)
        self.qk_group(s, g)

    def program(self):
        em = self.emit
        self.startup()
        NG = self.NC // 2
        order = [(s, g) for g in range(NG) for s in range(self.n_seq)]
        self.stage_a(*order[0])
        for idx, (s, g) in enumerate(order):
            kb = 2 * (g * self.n_seq + s)
            c0, c1 = 2 * g, 2 * g + 1
            for i, c in enumerate((c0, c1)):
                self.v_chunk(s, c)
                self.ktm_chunk(s, c)
                self.attn_chunk(s, c, kb + i)
                self.proj_ln2(s, c, kb + i)
            if idx + 1 < len(order):
                self.stage_a(*order[idx + 1])
            self.ln3_pair(s, kb)
            for i, c in enumerate((c0, c1)):
                self.ln3_mix(s, c, kb + i)
                self.r_chunk(s, c, kb + i)
            self.kff_group(s, g)
            for i, c in enumerate((c0, c1)):
                self.vff_ln4(s, c, kb + i)


def build_nc(n_seq=4, T=2048):
    b = Build(n_seq, T)
    nc = b.nc
    b.program()
    b.schedule()

    def phys(k):
        if k in ("t", "v", "s"):
            return {"t": "tensor", "v": "vector", "s": "scalar"}[k]
        return "sync" if (k.startswith("lq") or k.startswith("ts")) else "gpsimd"

    def replay(pk):
        def body(eng):
            for oek, fn, waits, inc in b.sched_ops:
                if phys(oek) != pk:
                    continue
                for (pe, val) in waits:
                    eng.wait_ge(b.sems[pe], val * b.mult[pe])
                inst = fn(eng)
                if inc:
                    inst.then_inc(b.sems[oek], b.mult[oek])
            for (wek, src_s, cnt) in b.final_waits:
                if wek == pk:
                    eng.wait_ge(b.sems[src_s], cnt * b.mult[src_s])
        return body

    with b.es:
        with nc.Block() as block:
            block.sync(replay("sync"))
            block.gpsimd(replay("gpsimd"))
            block.vector(replay("vector"))
            block.scalar(replay("scalar"))
            block.tensor(replay("tensor"))
    return nc


def _host_layout(x, w_attn, w_proj, W_r, W_k, W_v, mu_r, mu_k):
    n_cores = 8
    xs = np.ascontiguousarray(np.asarray(x, dtype=np.float32).reshape(n_cores, 4, 2048, E))
    wqT = np.ascontiguousarray(np.asarray(w_attn, dtype=np.float32).T)
    wpT = np.ascontiguousarray(np.asarray(w_proj, dtype=np.float32).T)
    wrT = np.ascontiguousarray(np.asarray(W_r, dtype=np.float32).T)
    wkT = np.ascontiguousarray(np.asarray(W_k, dtype=np.float32).T)
    wvT = np.ascontiguousarray(np.asarray(W_v, dtype=np.float32).T)
    mu = np.empty((128, 16), dtype=np.float32)
    mu[:, 0:8] = np.asarray(mu_r, dtype=np.float32).reshape(8, 128).T
    mu[:, 8:16] = np.asarray(mu_k, dtype=np.float32).reshape(8, 128).T
    ident = np.eye(128, dtype=np.float32)
    k_i = np.arange(128)[:, None]
    q_i = np.arange(128)[None, :]
    mask = (k_i <= q_i).astype(np.float32)
    mask4 = np.tile(mask, (1, 4))
    return xs, dict(wqT=wqT, wpT=wpT, wrT=wrT, wkT=wkT, wvT=wvT, mu_in=mu,
                    ident_in=ident, mask4_in=np.ascontiguousarray(mask4))


def kernel(L0_attn_x, L0_attn_w_attn, L0_attn_w_proj, L0_ffn_W_r, L0_ffn_W_k,
           L0_ffn_W_v, L0_ffn_mu_r, L0_ffn_mu_k):
    xs, shared = _host_layout(L0_attn_x, L0_attn_w_attn, L0_attn_w_proj,
                              L0_ffn_W_r, L0_ffn_W_k, L0_ffn_W_v,
                              L0_ffn_mu_r, L0_ffn_mu_k)
    nc = build_nc(4, 2048)
    in_maps = [dict(x=xs[i], **shared) for i in range(8)]
    res = run_bass_kernel_spmd(nc, in_maps, core_ids=list(range(8)))
    out = np.concatenate([np.asarray(r["out"]).reshape(4, 2048, E)
                          for r in res.results], axis=0)
    return out.astype(np.float32)


# version 25
# speedup vs baseline: 1.0454x; 1.0454x over previous
import numpy as np
from contextlib import ExitStack
import concourse.bass as bass
import concourse.mybir as mybir
from concourse.bass_utils import run_bass_kernel_spmd

F32 = mybir.dt.float32
BF16 = mybir.dt.bfloat16
I32 = mybir.dt.int32
AF = mybir.ActivationFunctionType
ALU = mybir.AluOpType
AX = mybir.AxisListType

E = 1024
NCC = 8
H = 16
D = 64
CH = 128
RSQRT_C = float(0x5F3759DF)


class Build:
    def __init__(self, n_seq=4, T=2048):
        self.n_seq = n_seq
        self.T = T
        self.NC = T // CH
        nc = bass.Bass("TRN2", target_bir_lowering=False)
        self.nc = nc
        self.es = ExitStack()
        self.x_d = nc.dram_tensor("x", [n_seq, T, E], F32, kind="ExternalInput").ap()
        self.out_d = nc.dram_tensor("out", [n_seq, T, E], F32, kind="ExternalOutput").ap()
        self.wq_d = nc.dram_tensor("wqT", [E, 3 * E], F32, kind="ExternalInput").ap()
        self.wp_d = nc.dram_tensor("wpT", [E, E], F32, kind="ExternalInput").ap()
        self.wr_d = nc.dram_tensor("wrT", [E, E], F32, kind="ExternalInput").ap()
        self.wk_d = nc.dram_tensor("wkT", [E, E], F32, kind="ExternalInput").ap()
        self.wv_d = nc.dram_tensor("wvT", [E, E], F32, kind="ExternalInput").ap()
        self.mu_d = nc.dram_tensor("mu_in", [128, 16], F32, kind="ExternalInput").ap()
        self.id_d = nc.dram_tensor("ident_in", [128, 128], F32, kind="ExternalInput").ap()
        self.mask_d = nc.dram_tensor("mask4_in", [128, 512], F32, kind="ExternalInput").ap()

        sb = lambda name, shape, dt: self.es.enter_context(nc.sbuf_tensor(name, shape, dt))
        ps = lambda name, shape, dt: self.es.enter_context(nc.psum_tensor(name, shape, dt))
        self.wq = sb("wq", [128, NCC * 3 * E], BF16)
        self.wp = sb("wp", [128, NCC * E], BF16)
        self.wr = sb("wr", [128, NCC * E], BF16)
        self.wk = sb("wk", [128, NCC * E], BF16)
        self.wv = sb("wv", [128, NCC * E], BF16)
        self.mu = sb("mu", [128, 16], F32)
        self.ident = sb("ident", [128, 128], BF16)
        self.mask4 = sb("mask4", [128, 512], BF16)
        self.xin = [sb(f"xin{i}", [128, E], F32) for i in range(3)]
        self.obuf = sb("obuf", [128, E], F32)
        self.x1 = [sb(f"x1_{i}", [128, E], F32) for i in range(2)]
        self.xn1tm = [sb(f"xn1tm{i}", [128, E], BF16) for i in range(2)]
        self.ln3tm = [sb(f"ln3tm{i}", [128, E], BF16) for i in range(2)]
        self.xn1fm = sb("xn1fm", [128, NCC * 256], BF16)
        self.qfm = sb("qfm", [128, NCC * 256], BF16)
        self.kfm = sb("kfm", [128, NCC * 256], BF16)
        self.ktm = sb("ktm", [128, E], BF16)
        self.vcaug = sb("vcaug", [128, 1040], BF16)
        self.sct = sb("sct", [128, H * CH], BF16)
        self.ytm = sb("ytm", [128, E], BF16)
        self.yfm = sb("yfm", [128, E], BF16)
        self.S16 = sb("S16", [128, 1040], BF16)
        self.xn3 = [sb(f"xn3_{i}", [128, NCC * 129], BF16) for i in range(2)]
        self.dbuf = sb("dbuf", [128, E], BF16)
        self.xrfm = sb("xrfm", [128, E], BF16)
        self.xkfm = sb("xkfm", [128, NCC * 256], BF16)
        self.kact = sb("kact", [128, NCC * 256], BF16)
        self.scr = [sb(f"scr{i}", [128, 512], F32) for i in range(3)]
        self.mubc = sb("mubc", [128, 2 * E], BF16)
        self.ones128 = sb("ones128", [128, 128], BF16)
        self.consts = sb("consts", [128, 2], F32)
        self.rs = [sb(f"rs{i}", [128, E], BF16) for i in range(2)]
        self.ffn = sb("ffn", [128, E], F32)
        self.stats = [sb(f"stats{i}", [128, 32], F32) for i in range(10)]
        self.pbt = [ps("pb0", [128, 512], F32), ps("pb1", [128, 512], F32),
                    ps("pw0", [128, 1024], F32), ps("pw1", [128, 1024], F32),
                    ps("pb2", [128, 512], F32)]
        self.slots = [(0, 0), (1, 0), (2, 0), (2, 512), (3, 0), (3, 512), (4, 0)]
        self.tb = [ps(f"tb{i}", [128, 1024], BF16) for i in range(1)]
        keys = (["t", "v", "s"] + [f"lq{i}" for i in range(4)]
                + [f"sg{i}" for i in range(4)] + [f"w{i}" for i in range(8)]
                + [f"ts{i}" for i in range(4)])
        self.sems = {k: self.es.enter_context(nc.semaphore(f"sem_{k}")) for k in keys}
        self.cnt = {k: 0 for k in self.sems}
        self.mult = {k: (1 if k in ("t", "v", "s") else 16) for k in keys}
        self.dma_engine = {k: ("q" if k.startswith("lq") else "g")
                           for k in keys if k not in ("t", "v", "s")}
        self.regions = {}
        self.waited = {a: {b: 0 for b in self.sems} for a in self.sems}
        self.pend = {k: False for k in self.sems}
        self._bank_rr = 0
        self._tmp_rr = 0
        self._scr_rr = 0
        self.ops = []
        self.final_waits = []

    def _reg(self, name):
        r = self.regions.get(name)
        if r is None:
            r = {"w": None, "r": set()}
            self.regions[name] = r
        return r

    def _need_wait(self, ek, pe, kind):
        if pe == ek and ek == "t":
            return False
        return True

    def emit(self, ek, fn, reads=(), writes=(), inc=True, unit=None, cost=None):
        oid = len(self.ops)
        deps = {}
        def add(pid, kind):
            if pid is None or pid == oid:
                return
            if kind == "raw" or pid not in deps:
                deps[pid] = kind
        for nm in reads:
            r = self._reg(nm)
            add(r["w"], "raw")
        for nm in writes:
            r = self._reg(nm)
            add(r["w"], "waw")
            for pid in r["r"]:
                add(pid, "war")
        if cost is None:
            cost = {"t": 150, "v": 330, "s": 650}.get(ek, 1400)
        self.ops.append(dict(id=oid, ek=ek, fn=fn, inc=inc, deps=deps,
                             unit=unit, cost=cost))
        for nm in writes:
            self.regions[nm] = {"w": oid, "r": set()}
        for nm in reads:
            self._reg(nm)["r"].add(oid)

    def take_pb(self):
        ti, off = self.slots[self._bank_rr % 7]
        self._bank_rr += 1
        return self.pbt[ti], off, f"pb{ti}_{off // 512}"

    def utok(self):
        self._utok = getattr(self, "_utok", 0) + 1
        return f"u{self._utok}"

    def take_scr(self):
        t = self._scr_rr % len(self.scr)
        self._scr_rr += 1
        return t


    @staticmethod
    def _phys(ek):
        if ek in ("t", "v", "s"):
            return ek
        return "q" if (ek.startswith("lq") or ek.startswith("ts")) else "g"

    def schedule(self):
        import heapq
        ops = self.ops
        units = []
        op_unit = [0] * len(ops)
        for op in ops:
            tok = op["unit"]
            if (tok is not None and units and units[-1]["tok"] == tok
                    and units[-1]["ek"] == op["ek"]):
                u = units[-1]
            else:
                u = dict(idx=len(units), tok=tok, ek=op["ek"], ops=[], cost=0.0)
                units.append(u)
            u["ops"].append(op)
            u["cost"] += op["cost"]
            op_unit[op["id"]] = u["idx"]
        n = len(units)
        succs = [[] for _ in range(n)]
        for u in units:
            d = {}
            for op in u["ops"]:
                for pid, kind in op["deps"].items():
                    pu = op_unit[pid]
                    if pu != u["idx"]:
                        if kind == "raw" or pu not in d:
                            d[pu] = kind
            u["deps"] = d
            for pu in d:
                succs[pu].append(u["idx"])
        LAT = 250.0
        prio = [0.0] * n
        for i in range(n - 1, -1, -1):
            m = 0.0
            for sx in succs[i]:
                m = max(m, prio[sx])
            prio[i] = units[i]["cost"] + m + (LAT if succs[i] else 0.0)
        ndep = [len(u["deps"]) for u in units]
        dep_fin = [0.0] * n
        finish = [0.0] * n
        free = {"t": 0.0, "v": 0.0, "s": 0.0, "q": 0.0, "g": 0.0}
        ready = [(-prio[i], i) for i in range(n) if ndep[i] == 0]
        heapq.heapify(ready)
        order = []
        K = 40
        while ready:
            cand = []
            best = None
            best_key = None
            for _ in range(min(K, len(ready))):
                item = heapq.heappop(ready)
                i = item[1]
                pe = self._phys(units[i]["ek"])
                est = max(free[pe], dep_fin[i])
                key = (est, item[0])
                if best is None or key < best_key:
                    if best is not None:
                        cand.append(best)
                    best, best_key = item, key
                else:
                    cand.append(item)
            for it in cand:
                heapq.heappush(ready, it)
            i = best[1]
            u = units[i]
            pe = self._phys(u["ek"])
            start = max(free[pe], dep_fin[i])
            f = start + u["cost"]
            finish[i] = f
            free[pe] = f
            order.append(i)
            for sx in succs[i]:
                ndep[sx] -= 1
                dep_fin[sx] = max(dep_fin[sx], f + LAT)
                if ndep[sx] == 0:
                    heapq.heappush(ready, (-prio[sx], sx))
        assert len(order) == n, f"scheduled {len(order)} of {n} units (cycle?)"
        self.sched_makespan_us = max(finish) / 1000 if n else 0
        key_cnt = {k: 0 for k in self.sems}
        mark = [None] * n
        pw = {e: {k: 0 for k in self.sems} for e in ("t", "v", "s", "q", "g")}
        sched_ops = []
        for i in order:
            u = units[i]
            pe = self._phys(u["ek"])
            req = {}
            for pu, kind in u["deps"].items():
                mk = mark[pu]
                assert mk is not None
                sk, cn = mk
                if not self._need_wait(u["ek"], sk, kind):
                    continue
                if cn > req.get(sk, 0):
                    req[sk] = cn
            waits = [(sk, cn) for sk, cn in req.items() if cn > pw[pe][sk]]
            for sk, cn in waits:
                pw[pe][sk] = cn
            first = True
            for op in u["ops"]:
                sched_ops.append((op["ek"], op["fn"], waits if first else [], op["inc"]))
                first = False
                if op["inc"]:
                    key_cnt[op["ek"]] += 1
            assert u["ops"][-1]["inc"], "unit must end with an inc'd op"
            mark[i] = (u["ek"], key_cnt[u["ek"]])
        self.sched_ops = sched_ops
        self.final_waits = [("gpsimd", f"sg{i}", key_cnt[f"sg{i}"]) for i in range(4)]

    def startup(self):
        em = self.emit
        def wload(dst, src, cols):
            src_r = src.rearrange("(cc p) d -> p cc d", p=128)
            dst_r = dst.ap().rearrange("p (cc d) -> p cc d", d=cols)
            return src_r, dst_r
        for i, (dst, src, cols, nm) in enumerate(((self.wq, self.wq_d, 3 * E, "wq"),
                                                   (self.wp, self.wp_d, E, "wp"),
                                                   (self.wr, self.wr_d, E, "wr"),
                                                   (self.wk, self.wk_d, E, "wk"),
                                                   (self.wv, self.wv_d, E, "wv"))):
            s_r, d_r = wload(dst, src, cols)
            em(f"w{i}", lambda e, d_r=d_r, s_r=s_r: e.dma_start(out=d_r, in_=s_r),
               writes=[nm])
        em("w5", lambda e: e.dma_start(out=self.mu[:], in_=self.mu_d), writes=["mu"])
        em("w6", lambda e: e.dma_start(out=self.ident[:], in_=self.id_d), writes=["ident"])
        em("w7", lambda e: e.dma_start(out=self.mask4[:], in_=self.mask_d), writes=["mask4"])
        ones_ap = self.vcaug.ap().rearrange("p (h x) -> p h x", x=65)[:, :, 64:65]
        em("v", lambda e: e.memset(ones_ap, 1.0), writes=["vcaug_ones"])
        em("v", lambda e: e.memset(self.ones128[:], 1.0), writes=["ones128"])
        em("v", lambda e: e.memset(self.consts[:, 0:1], 1.5), writes=["consts"])
        for m in range(NCC):
            em("v", lambda e, m=m: e.tensor_scalar(
                self.mubc[:, 128 * m:128 * m + 128], self.ones128[:],
                self.mu[:, m:m + 1], None, ALU.mult),
               reads=["ones128", "mu"], writes=["mubc"])
            em("v", lambda e, m=m: e.tensor_scalar(
                self.mubc[:, E + 128 * m:E + 128 * m + 128], self.ones128[:],
                self.mu[:, 8 + m:8 + m + 1], None, ALU.mult),
               reads=["ones128", "mu"], writes=["mubc"])
        em("s", lambda e: e.preload_act_table(AF.Exp), inc=True)

    def ln(self, src_ap_halves, src_regions, st_i, extra_reads=()):
        em = self.emit
        st = self.stats[st_i]
        nm = f"st{st_i}"
        reads = list(src_regions) + list(extra_reads)
        n = len(src_ap_halves)
        for i, h in enumerate(src_ap_halves):
            em("v", lambda e, i=i, h=h: e.bn_stats(st[:, 6 * i:6 * i + 6], h),
               reads=reads, writes=[nm] if i == 0 else [nm], inc=True)
        em("v", lambda e: e.bn_aggr(st[:, 24:26], st[:, 0:6 * n]), reads=[nm], writes=[nm])
        z, y, t, u = st[:, 26:27], st[:, 27:28], st[:, 28:29], st[:, 29:30]
        em("v", lambda e: e.tensor_scalar(z, st[:, 25:26], 1e-5, None, ALU.add),
           reads=[nm], writes=[nm])
        em("v", lambda e: e.tensor_scalar(y.bitcast(I32), z.bitcast(I32), -0.5,
                                          RSQRT_C, ALU.mult, ALU.add),
           reads=[nm], writes=[nm])
        for _ in range(2):
            em("v", lambda e: e.tensor_scalar(t, y, y, z, ALU.mult, ALU.mult),
               reads=[nm], writes=[nm], cost=300)
            em("v", lambda e: e.tensor_scalar(u, t, -0.5, 1.5, ALU.mult, ALU.add),
               reads=[nm], writes=[nm], cost=300)
            em("v", lambda e: e.tensor_scalar(y, u, y, None, ALU.mult),
               reads=[nm], writes=[nm], cost=300)
        em("v", lambda e: e.scalar_tensor_tensor(st[:, 30:31], y, -1.0, st[:, 24:25],
                                                 ALU.mult, ALU.mult),
           reads=[nm], writes=[nm])
        return y, st[:, 30:31], nm

    def load_chunk(self, s, c):
        xb = self.xin[c % 3]
        src = self.x_d[s, c * CH:(c + 1) * CH, :]
        self.emit(f"lq{c % 3}", lambda e: e.dma_start(out=xb[:], in_=src),
                  writes=[f"xin{c % 3}"], cost=1700)

    def ln1_and_transpose(self, s, c):
        em = self.emit
        xb = self.xin[c % 3]
        xr = f"xin{c % 3}"
        rstd, bias, stnm = self.ln([xb[:, 0:512], xb[:, 512:1024]], [xr], 0 + c % 2)
        xt = self.xn1tm[c % 2]
        xtnm = f"xn1tm{c % 2}"
        em("s", lambda e: e.activation(xt[:], xb[:], AF.Identity, bias=bias, scale=rstd),
           reads=[xr, stnm], writes=[xtnm], cost=1100)
        off = 128 * (c % 2)
        out_ap = self.xn1fm.ap().rearrange("p (cc t) -> p cc t", t=256)[:, :, off:off + 128]
        in_ap = xt.ap().rearrange("p (cc t) -> p cc t", t=128)
        em(f"ts{c % 2}", lambda e, out_ap=out_ap, in_ap=in_ap: e.dma_start_transpose(out_ap, in_ap),
           reads=[xtnm], writes=[f"xn1fm_{c % 2}"], cost=1900)

    def qk_group(self, s, g):
        em = self.emit
        for tensor_i, (dst, dstnm) in enumerate(((self.qfm, "qfm"), (self.kfm, "kfm"))):
            for pair in range(NCC // 2):
                bank, boff, bnm = self.take_pb()
                tok = self.utok()
                for half in range(2):
                    m = 2 * pair + half
                    for cc in range(NCC):
                        lhsT = self.wq[:, cc * 3072 + tensor_i * E + 128 * m:
                                       cc * 3072 + tensor_i * E + 128 * m + 128]
                        rhs = self.xn1fm[:, cc * 256:cc * 256 + 256]
                        em("t", lambda e, o=bank[:, boff + 256 * half:boff + 256 * half + 256],
                           l=lhsT, r=rhs, cc=cc:
                           e.matmul(o, l, r, start=(cc == 0), stop=(cc == NCC - 1)),
                           reads=["wq", "xn1fm_0", "xn1fm_1"], writes=[bnm],
                           inc=(cc == NCC - 1), unit=tok, cost=145)
                rg = [f"{dstnm}_{2 * pair}", f"{dstnm}_{2 * pair + 1}"]
                s0 = self.take_scr()
                r0 = self.scr[s0]
                em("s", lambda e, dst=dst, pair=pair, bank=bank, boff=boff: e.activation(
                    dst[:, 512 * pair:512 * pair + 512], bank[:, boff:boff + 512], AF.Relu),
                   reads=[bnm], writes=rg, cost=600)
                em("v", lambda e, r0=r0, dst=dst, pair=pair, bank=bank, boff=boff:
                   e.tensor_tensor(r0[:], bank[:, boff:boff + 512],
                                   dst[:, 512 * pair:512 * pair + 512], ALU.subtract),
                   reads=[bnm] + rg, writes=[f"scr{s0}"], cost=560)
                em("s", lambda e, r0=r0: e.activation(r0[:], r0[:], AF.Exp),
                   reads=[f"scr{s0}"], writes=[f"scr{s0}"], cost=600)
                em("v", lambda e, r0=r0, dst=dst, pair=pair: e.tensor_add(
                    dst[:, 512 * pair:512 * pair + 512],
                    dst[:, 512 * pair:512 * pair + 512], r0[:]),
                   reads=[f"scr{s0}"] + rg, writes=rg, cost=430)

    def v_chunk(self, s, c):
        em = self.emit
        off = 128 * (c % 2)
        for half in range(2):
            bank, boff, bnm = self.take_pb()
            tok = self.utok()
            for cc in range(NCC):
                lhsT = self.xn1fm[:, cc * 256 + off:cc * 256 + off + 128]
                rhs = self.wq[:, cc * 3072 + 2 * E + half * 512:
                              cc * 3072 + 2 * E + half * 512 + 512]
                em("t", lambda e, o=bank[:, boff:boff + 512], l=lhsT, r=rhs, cc=cc:
                   e.matmul(o, l, r, start=(cc == 0), stop=(cc == NCC - 1)),
                   reads=["wq", f"xn1fm_{c % 2}"], writes=[bnm], inc=(cc == NCC - 1),
                   unit=tok, cost=245)
            in_ap = bank[:, boff:boff + 512].rearrange("p (h x) -> p h x", x=64)
            out_ap = self.vcaug.ap().rearrange("p (h x) -> p h x", x=65)[
                :, 8 * half:8 * half + 8, 0:64]
            em("s", lambda e, out_ap=out_ap, in_ap=in_ap: e.copy(out_ap, in_ap),
               reads=[bnm], writes=["vcaug"])

    def ktm_chunk(self, s, c):
        em = self.emit
        off = 128 * (c % 2)
        tbank = self.tb[0]
        tok = self.utok()
        for m in range(NCC):
            em("t", lambda e, m=m: e.transpose(tbank[:, 128 * m:128 * m + 128],
                                               self.kfm[:, 256 * m + off:256 * m + off + 128],
                                               self.ident[:]),
               reads=[f"kfm_{m}", "ident"], writes=["tb0"], inc=(m == NCC - 1),
               unit=tok, cost=130)
        em("v", lambda e, tbank=tbank: e.tensor_copy(self.ktm[:], tbank[:, 0:1024]),
           reads=["tb0"], writes=["ktm"], cost=700)

    def attn_chunk(self, s, c):
        em = self.emit
        off = 128 * (c % 2)

        def qk_slice(t, h):
            base = 64 * (h % 2)
            col = 256 * (h // 2) + off
            return t[base:base + 64, col:col + 128]

        for hbase in (0, 8):
            sl = [self.take_pb(), self.take_pb()]
            tok = self.utok()
            for j in range(4):
                for par in range(2):
                    h = hbase + 2 * j + par
                    bank, boff, bnm = sl[par]
                    em("t", lambda e, h=h, j=j, bank=bank, boff=boff: e.matmul(
                        bank[:, boff + 128 * j:boff + 128 * j + 128],
                        qk_slice(self.kfm, h), qk_slice(self.qfm, h),
                        start=True, stop=True),
                       reads=[f"kfm_{h // 2}", f"qfm_{h // 2}"], writes=[bnm],
                       inc=(j == 3 and par == 1), unit=tok, cost=70)
            for par in range(2):
                bank, boff, bnm = sl[par]
                h0 = hbase + par
                out_ap = self.sct.ap().rearrange("p (a b) -> p a b", b=128)[:, h0:h0 + 7:2, :]
                em("v", lambda e, out_ap=out_ap, bank=bank, boff=boff: e.tensor_mul(
                    out_ap, bank[:, boff:boff + 512], self.mask4[:]),
                   reads=[bnm, "mask4"], writes=["sct"])
        first = (c == 0)
        for half in range(2):
            bank = self.pbt[2 + half]
            rg0, rg1 = f"pb{2 + half}_0", f"pb{2 + half}_1"
            tok = self.utok()
            for j in range(8):
                h = 8 * half + j
                ocol = 512 * (j // 4) + 66 * (j % 4)
                wnm = rg0 if j < 4 else rg1
                em("t", lambda e, h=h, ocol=ocol, bank=bank: e.matmul(
                    bank[:, ocol:ocol + 65],
                    self.sct[:, 128 * h:128 * h + 128],
                    self.vcaug[:, 65 * h:65 * h + 65],
                    start=True, stop=first),
                   reads=["sct", "vcaug", "vcaug_ones"], writes=[wnm],
                   inc=(first and j == 7), unit=tok, cost=100)
                if not first:
                    base = 64 * (h % 2)
                    scol = 130 * (h // 2) + 65 * (h % 2)
                    em("t", lambda e, h=h, ocol=ocol, base=base, scol=scol, bank=bank: e.matmul(
                        bank[:, ocol:ocol + 65],
                        qk_slice(self.qfm, h),
                        self.S16[base:base + 64, scol:scol + 65],
                        start=False, stop=True),
                       reads=[f"qfm_{h // 2}", "S16"], writes=[wnm], inc=(j == 7),
                       unit=tok, cost=100)
            st = self.stats[8 + c % 2]
            stnm4 = f"st{8 + c % 2}"
            den_ap = bank.ap().rearrange("p (a x) -> p a x", x=512)[:, :, 0:264].rearrange(
                "p a (j x) -> p a j x", x=66)[:, :, :, 64:65]
            em("v", lambda e, den_ap=den_ap, st=st, half=half: e.tensor_scalar(
                st[:, 8 * half:8 * half + 8], den_ap, 1e-6, None, ALU.add),
               reads=[rg0, rg1], writes=[stnm4], cost=310)
            em("v", lambda e, st=st, half=half: e.reciprocal(
                st[:, 16 + 8 * half:24 + 8 * half], st[:, 8 * half:8 * half + 8]),
               reads=[stnm4], writes=[stnm4], cost=310)
            num_ap = bank.ap().rearrange("p (a x) -> p a x", x=512)[:, :, 0:264].rearrange(
                "p a (j x) -> p a j x", x=66)[:, :, :, 0:64]
            rec_ap = st[:, 16 + 8 * half:24 + 8 * half].unsqueeze(2).broadcast_to([128, 8, 64])
            em("v", lambda e, half=half, num_ap=num_ap, rec_ap=rec_ap: e.tensor_tensor(
                self.ytm.ap().rearrange("p (j x) -> p j x", x=64)[
                    :, 8 * half:8 * half + 8, :],
                num_ap, rec_ap, ALU.mult),
               reads=[rg0, rg1, stnm4], writes=["ytm"], cost=700)
        for pg, pairs in enumerate(((0, 1, 2), (3, 4, 5), (6, 7))):
            bank, boff, bnm = self.take_pb()
            tok = self.utok()
            for i, p in enumerate(pairs):
                em("t", lambda e, p=p, i=i, bank=bank, boff=boff: e.matmul(
                    bank[:, boff + 130 * i:boff + 130 * i + 130],
                    self.ktm[:, 128 * p:128 * p + 128],
                    self.vcaug[:, 130 * p:130 * p + 130],
                    start=True, stop=True),
                   reads=["ktm", "vcaug", "vcaug_ones"], writes=[bnm],
                   inc=(i == len(pairs) - 1), unit=tok, cost=120)
            w = 130 * len(pairs)
            colS = 130 * pairs[0]
            if first:
                em("v", lambda e, w=w, colS=colS, bank=bank, boff=boff: e.tensor_copy(
                    self.S16[:, colS:colS + w], bank[:, boff:boff + w]),
                   reads=[bnm], writes=["S16"], cost=500)
            else:
                em("v", lambda e, w=w, colS=colS, bank=bank, boff=boff: e.tensor_add(
                    self.S16[:, colS:colS + w], self.S16[:, colS:colS + w],
                    bank[:, boff:boff + w]),
                   reads=[bnm, "S16"], writes=["S16"], cost=560)
        em("ts3", lambda e: e.dma_start_transpose(
            self.yfm.ap().rearrange("p (m f) -> p m f", f=128),
            self.ytm.ap().rearrange("p (m f) -> p m f", f=128)),
           reads=["ytm"], writes=["yfm"], cost=1900)

    def proj_ln2(self, s, c):
        em = self.emit
        sl0, sl1 = self.take_pb(), self.take_pb()
        for half, sl in ((0, sl0), (1, sl1)):
            bank, boff, bnm = sl
            tok = self.utok()
            for m in range(NCC):
                em("t", lambda e, m=m, half=half, o=bank[:, boff:boff + 512]: e.matmul(
                    o, self.yfm[:, 128 * m:128 * m + 128],
                    self.wp[:, m * E + half * 512:m * E + half * 512 + 512],
                    start=(m == 0), stop=(m == NCC - 1)),
                   reads=["yfm", "wp"], writes=[bnm], inc=(m == NCC - 1),
                   unit=tok, cost=245)
        rstd, bias, stnm = self.ln([sl0[0][:, sl0[1]:sl0[1] + 512],
                                    sl1[0][:, sl1[1]:sl1[1] + 512]],
                                   [sl0[2], sl1[2]], 2 + c % 2)
        x1b = self.x1[c % 2]
        em("s", lambda e: e.activation(x1b[:, 0:512], sl0[0][:, sl0[1]:sl0[1] + 512],
                                       AF.Identity, bias=bias, scale=rstd),
           reads=[sl0[2], stnm], writes=[f"x1_{c % 2}"])
        em("s", lambda e: e.activation(x1b[:, 512:1024], sl1[0][:, sl1[1]:sl1[1] + 512],
                                       AF.Identity, bias=bias, scale=rstd),
           reads=[sl1[2], stnm], writes=[f"x1_{c % 2}"])
        em("v", lambda e: e.tensor_add(x1b[:], x1b[:], self.xin[c % 3][:]),
           reads=[f"x1_{c % 2}", f"xin{c % 3}"], writes=[f"x1_{c % 2}"])

    def ln3_mix(self, s, c):
        em = self.emit
        x1b = self.x1[c % 2]
        x1r = f"x1_{c % 2}"
        rstd, bias, stnm = self.ln([x1b[:, 0:512], x1b[:, 512:1024]], [x1r], 4 + c % 2)
        xt = self.ln3tm[c % 2]
        xtnm = f"ln3tm{c % 2}"
        em("s", lambda e: e.activation(xt[:], x1b[:], AF.Identity, bias=bias, scale=rstd),
           reads=[x1r, stnm], writes=[xtnm], cost=1100)
        tbank = self.tb[0]
        tok = self.utok()
        for m in range(NCC):
            em("t", lambda e, m=m: e.transpose(tbank[:, 128 * m:128 * m + 128],
                                               xt[:, 128 * m:128 * m + 128], self.ident[:]),
               reads=[xtnm, "ident"], writes=["tb0"], inc=(m == NCC - 1),
               unit=tok, cost=130)
        xb = self.xn3[c % 2]
        xbnm = f"xn3_{c % 2}"
        out_ap = xb.ap().rearrange("p (cc t) -> p cc t", t=129)[:, :, 1:129]
        in_ap = tbank.ap().rearrange("p (cc t) -> p cc t", t=128)
        em("v", lambda e: e.tensor_copy(out_ap, in_ap), reads=["tb0"], writes=[xbnm])
        seam = xb.ap().rearrange("p (cc t) -> p cc t", t=129)[:, :, 0:1]
        if c == 0:
            em("v", lambda e: e.memset(seam, 0.0), writes=[xbnm])
        else:
            pv = self.xn3[(c - 1) % 2]
            prev_last = pv.ap().rearrange("p (cc t) -> p cc t", t=129)[:, :, 128:129]
            em("v", lambda e: e.tensor_copy(seam, prev_last),
               reads=[f"xn3_{(c - 1) % 2}"], writes=[xbnm])
        cur = xb.ap().rearrange("p (cc t) -> p cc t", t=129)[:, :, 1:129]
        prev = xb.ap().rearrange("p (cc t) -> p cc t", t=129)[:, :, 0:128]
        em("v", lambda e: e.tensor_tensor(
            self.dbuf.ap().rearrange("p (cc t) -> p cc t", t=128), cur, prev, ALU.subtract),
           reads=[xbnm], writes=["dbuf"])
        off = 128 * (c % 2)
        prev_all = xb.ap().rearrange("p (cc t) -> p cc t", t=129)[:, :, 0:128]
        xk_ap = self.xkfm.ap().rearrange("p (cc t) -> p cc t", t=256)[:, :, off:off + 128]
        em("v", lambda e: e.tensor_mul(self.xrfm[:], self.dbuf[:], self.mubc[:, 0:E]),
           reads=["dbuf", "mubc"], writes=["xrfm"])
        em("v", lambda e, prev_all=prev_all: e.tensor_add(
            self.xrfm[:], self.xrfm[:], prev_all),
           reads=["xrfm", xbnm], writes=["xrfm"])
        em("v", lambda e, xk_ap=xk_ap: e.tensor_tensor(
            xk_ap, self.dbuf[:], self.mubc[:, E:2 * E], ALU.mult),
           reads=["dbuf", "mubc"], writes=["xkfm"])
        em("v", lambda e, xk_ap=xk_ap, prev_all=prev_all: e.tensor_tensor(
            xk_ap, xk_ap, prev_all, ALU.add),
           reads=["xkfm", xbnm], writes=["xkfm"])

    def r_chunk(self, s, c):
        em = self.emit
        sl0, sl1 = self.take_pb(), self.take_pb()
        for half, sl in ((0, sl0), (1, sl1)):
            bank, boff, bnm = sl
            tok = self.utok()
            for m in range(NCC):
                em("t", lambda e, m=m, half=half, o=bank[:, boff:boff + 512]: e.matmul(
                    o, self.xrfm[:, 128 * m:128 * m + 128],
                    self.wr[:, m * E + half * 512:m * E + half * 512 + 512],
                    start=(m == 0), stop=(m == NCC - 1)),
                   reads=["xrfm", "wr"], writes=[bnm], inc=(m == NCC - 1),
                   unit=tok, cost=245)
        rb = self.rs[c % 2]
        for half, sl in ((0, sl0), (1, sl1)):
            em("s", lambda e, half=half, sl=sl: e.activation(
                rb[:, 512 * half:512 * half + 512], sl[0][:, sl[1]:sl[1] + 512],
                AF.Tanh, scale=0.5),
               reads=[sl[2]], writes=[f"rs{c % 2}"])
        em("v", lambda e: e.tensor_scalar(rb[:], rb[:], 0.5, 0.5, ALU.mult, ALU.add),
           reads=[f"rs{c % 2}"], writes=[f"rs{c % 2}"])

    def kff_group(self, s, g):
        em = self.emit
        for pair in range(NCC // 2):
            bank, boff, bnm = self.take_pb()
            tok = self.utok()
            for half in range(2):
                m = 2 * pair + half
                for cc in range(NCC):
                    em("t", lambda e, m=m, cc=cc,
                       o=bank[:, boff + 256 * half:boff + 256 * half + 256]: e.matmul(
                        o, self.wk[:, cc * E + 128 * m:cc * E + 128 * m + 128],
                        self.xkfm[:, cc * 256:cc * 256 + 256],
                        start=(cc == 0), stop=(cc == NCC - 1)),
                       reads=["wk", "xkfm"], writes=[bnm], inc=(cc == NCC - 1),
                       unit=tok, cost=145)
            s0 = self.take_scr()
            r0 = self.scr[s0]
            em("s", lambda e, r0=r0, bank=bank, boff=boff: e.activation(
                r0[:], bank[:, boff:boff + 512], AF.Relu),
               reads=[bnm], writes=[f"scr{s0}"])
            em("v", lambda e, r0=r0, pair=pair: e.tensor_mul(
                self.kact[:, 512 * pair:512 * pair + 512], r0[:], r0[:]),
               reads=[f"scr{s0}"], writes=["kact"])

    def vff_ln4(self, s, c):
        em = self.emit
        off = 128 * (c % 2)
        sl0, sl1 = self.take_pb(), self.take_pb()
        for half, sl in ((0, sl0), (1, sl1)):
            bank, boff, bnm = sl
            tok = self.utok()
            for m in range(NCC):
                em("t", lambda e, m=m, half=half, o=bank[:, boff:boff + 512]: e.matmul(
                    o, self.kact[:, 256 * m + off:256 * m + off + 128],
                    self.wv[:, m * E + half * 512:m * E + half * 512 + 512],
                    start=(m == 0), stop=(m == NCC - 1)),
                   reads=["kact", "wv"], writes=[bnm], inc=(m == NCC - 1),
                   unit=tok, cost=245)
        rb = self.rs[c % 2]
        for half, sl in ((0, sl0), (1, sl1)):
            em("v", lambda e, half=half, sl=sl: e.tensor_mul(
                self.ffn[:, 512 * half:512 * half + 512], rb[:, 512 * half:512 * half + 512],
                sl[0][:, sl[1]:sl[1] + 512]),
               reads=[f"rs{c % 2}", sl[2]], writes=["ffn"], cost=1100)
        rstd, bias, stnm = self.ln([self.ffn[:, 0:512], self.ffn[:, 512:1024]], ["ffn"], 6 + c % 2)
        ob = self.obuf
        em("s", lambda e: e.activation(ob[:], self.ffn[:], AF.Identity,
                                       bias=bias, scale=rstd),
           reads=["ffn", stnm], writes=["obuf"])
        em("v", lambda e: e.tensor_add(ob[:], ob[:], self.x1[c % 2][:]),
           reads=["obuf", f"x1_{c % 2}"], writes=["obuf"])
        dst = self.out_d[s, c * CH:(c + 1) * CH, :]
        self.emit(f"sg{c % 2}", lambda e: e.dma_start(out=dst, in_=ob[:]),
                  reads=["obuf"], cost=1700)

    def stage_a(self, s, g):
        c0, c1 = 2 * g, 2 * g + 1
        self.load_chunk(s, c0)
        self.load_chunk(s, c1)
        self.ln1_and_transpose(s, c0)
        self.ln1_and_transpose(s, c1)
        self.qk_group(s, g)

    def program(self):
        em = self.emit
        self.startup()
        NG = self.NC // 2
        self.stage_a(0, 0)
        for s in range(self.n_seq):
            for g in range(NG):
                c0, c1 = 2 * g, 2 * g + 1
                for c in (c0, c1):
                    self.v_chunk(s, c)
                    self.ktm_chunk(s, c)
                    self.attn_chunk(s, c)
                    self.proj_ln2(s, c)
                if g + 1 < NG:
                    self.stage_a(s, g + 1)
                elif s + 1 < self.n_seq:
                    self.stage_a(s + 1, 0)
                for c in (c0, c1):
                    self.ln3_mix(s, c)
                    self.r_chunk(s, c)
                self.kff_group(s, g)
                for c in (c0, c1):
                    self.vff_ln4(s, c)


def build_nc(n_seq=4, T=2048):
    b = Build(n_seq, T)
    nc = b.nc
    b.program()
    b.schedule()

    def phys(k):
        if k in ("t", "v", "s"):
            return {"t": "tensor", "v": "vector", "s": "scalar"}[k]
        return "sync" if (k.startswith("lq") or k.startswith("ts")) else "gpsimd"

    def replay(pk):
        def body(eng):
            for oek, fn, waits, inc in b.sched_ops:
                if phys(oek) != pk:
                    continue
                for (pe, val) in waits:
                    eng.wait_ge(b.sems[pe], val * b.mult[pe])
                inst = fn(eng)
                if inc:
                    inst.then_inc(b.sems[oek], b.mult[oek])
            for (wek, src_s, cnt) in b.final_waits:
                if wek == pk:
                    eng.wait_ge(b.sems[src_s], cnt * b.mult[src_s])
        return body

    with b.es:
        with nc.Block() as block:
            block.sync(replay("sync"))
            block.gpsimd(replay("gpsimd"))
            block.vector(replay("vector"))
            block.scalar(replay("scalar"))
            block.tensor(replay("tensor"))
    return nc


def _host_layout(x, w_attn, w_proj, W_r, W_k, W_v, mu_r, mu_k):
    n_cores = 8
    xs = np.ascontiguousarray(np.asarray(x, dtype=np.float32).reshape(n_cores, 4, 2048, E))
    wqT = np.ascontiguousarray(np.asarray(w_attn, dtype=np.float32).T)
    wpT = np.ascontiguousarray(np.asarray(w_proj, dtype=np.float32).T)
    wrT = np.ascontiguousarray(np.asarray(W_r, dtype=np.float32).T)
    wkT = np.ascontiguousarray(np.asarray(W_k, dtype=np.float32).T)
    wvT = np.ascontiguousarray(np.asarray(W_v, dtype=np.float32).T)
    mu = np.empty((128, 16), dtype=np.float32)
    mu[:, 0:8] = np.asarray(mu_r, dtype=np.float32).reshape(8, 128).T
    mu[:, 8:16] = np.asarray(mu_k, dtype=np.float32).reshape(8, 128).T
    ident = np.eye(128, dtype=np.float32)
    k_i = np.arange(128)[:, None]
    q_i = np.arange(128)[None, :]
    mask = (k_i <= q_i).astype(np.float32)
    mask4 = np.tile(mask, (1, 4))
    return xs, dict(wqT=wqT, wpT=wpT, wrT=wrT, wkT=wkT, wvT=wvT, mu_in=mu,
                    ident_in=ident, mask4_in=np.ascontiguousarray(mask4))


def kernel(L0_attn_x, L0_attn_w_attn, L0_attn_w_proj, L0_ffn_W_r, L0_ffn_W_k,
           L0_ffn_W_v, L0_ffn_mu_r, L0_ffn_mu_k):
    xs, shared = _host_layout(L0_attn_x, L0_attn_w_attn, L0_attn_w_proj,
                              L0_ffn_W_r, L0_ffn_W_k, L0_ffn_W_v,
                              L0_ffn_mu_r, L0_ffn_mu_k)
    nc = build_nc(4, 2048)
    in_maps = [dict(x=xs[i], **shared) for i in range(8)]
    res = run_bass_kernel_spmd(nc, in_maps, core_ids=list(range(8)))
    out = np.concatenate([np.asarray(r["out"]).reshape(4, 2048, E)
                          for r in res.results], axis=0)
    return out.astype(np.float32)
